# version 05 (final)
# baseline (speedup 1.0000x reference)
import numpy as np
from contextlib import ExitStack

import concourse.bass as bass
import concourse.mybir as mybir
from concourse.bass_utils import run_bass_kernel_spmd

F32 = mybir.dt.float32
AF = mybir.ActivationFunctionType
ALU = mybir.AluOpType
AX = mybir.AxisListType

D = 1024
T = 2048
NG = T // 512
NTC = T // 128
NC8 = D // 128
NE = 8
NFC = 4
NXP = NE + 2
NH = 16
HD = 64
SEQ = 4096
NKC = SEQ // 128
VSLOT = 66
VW = NH * VSLOT
KVW = D + VW
EPS = 1e-5

DEBUG_TAPS = ()
PHASES = 99
SERIAL = False


class Ctr:
    def __init__(self, ctx, nc, name):
        self.h = ctx.enter_context(nc.semaphore(name))
        self.v = 0

    def inc(self, inst, n=1):
        inst.then_inc(self.h, n)
        self.v += n
        return self.v

    def wait(self, eng, v=None):
        eng.wait_ge(self.h, self.v if v is None else v)


def build_program(n_cores=8, taps=DEBUG_TAPS, phases=PHASES, serial=None):
    if serial is None:
        serial = SERIAL
    nc = bass.Bass("TRN2", target_bir_lowering=False, debug=False,
                   detect_race_conditions=False, dynamic_dma_scratch_size=8192)

    xT_d = nc.dram_tensor("xT", [D, T], F32, kind="ExternalInput").ap()
    wq_d = nc.dram_tensor("wq_t", [8, 128, 8, 128], F32, kind="ExternalInput").ap()
    wo_d = nc.dram_tensor("wo_t", [8, 128, 8, 128], F32, kind="ExternalInput").ap()
    wk_d = nc.dram_tensor("wk_m", [8, 128, 1024], F32, kind="ExternalInput").ap()
    wv_d = nc.dram_tensor("wv_m", [8, 128, 1024], F32, kind="ExternalInput").ap()
    rw_d = nc.dram_tensor("rw_t", [128, 8, 8], F32, kind="ExternalInput").ap()
    ropeKc_d = nc.dram_tensor("ropeKc", [128, 1024], F32, kind="ExternalInput").ap()
    ropeKs_d = nc.dram_tensor("ropeKs", [128, 1024], F32, kind="ExternalInput").ap()
    ropeQc_d = nc.dram_tensor("ropeQc", [4, 128, 512], F32, kind="ExternalInput").ap()
    ropeQs_d = nc.dram_tensor("ropeQs", [4, 128, 512], F32, kind="ExternalInput").ap()
    wg_d = nc.dram_tensor("wg_t", [NXP, NFC, 128, 8, 128], F32, kind="ExternalInput").ap()
    wu_d = nc.dram_tensor("wu_t", [NXP, NFC, 128, 8, 128], F32, kind="ExternalInput").ap()
    wd_d = nc.dram_tensor("wd_t", [NXP, 128, NFC, 1024], F32, kind="ExternalInput").ap()
    ones2_d = nc.dram_tensor("ones2", [2, 128], F32, kind="ExternalInput").ap()
    onesc_d = nc.dram_tensor("onesc", [128, 2], F32, kind="ExternalInput").ap()
    e2_d = nc.dram_tensor("e2", [128, 2], F32, kind="ExternalInput").ap()
    id_d = nc.dram_tensor("ident", [128, 128], F32, kind="ExternalInput").ap()
    out_d = nc.dram_tensor("outT", [D, T], F32, kind="ExternalOutput").ap()

    qT_d = nc.dram_tensor("qT_s", [D, T], F32).ap()
    kvin_d = nc.dram_tensor("kvin_s", [T, KVW], F32).ap()
    kvout_d = nc.dram_tensor("kvout_s", [SEQ, KVW], F32).ap()
    arin_d = nc.dram_tensor("arin_s", [1, 1], F32).ap()
    arout_d = nc.dram_tensor("arout_s", [1, 1], F32).ap()

    groups = [[2 * i, 2 * i + 1] for i in range(max(1, n_cores // 2))]

    tapspec = {
        "y1": [D, T], "tw": [128, NTC, 8], "xmid": [D, T], "q": [D, T],
        "kv": [T, KVW], "kvout": [SEQ, KVW], "mneg": [128, 1], "xattn": [D, T],
        "oT": [D, T], "y2": [D, T],
    }
    tap_d = {t: nc.dram_tensor("tap_" + t, tapspec[t], F32, kind="ExternalOutput").ap()
             for t in taps}

    ctx = ExitStack()
    with ctx:
        sbx = lambda name, shape, dt=F32: ctx.enter_context(nc.sbuf_tensor(name, shape, dt))
        psb = lambda name, shape: ctx.enter_context(nc.psum_tensor(name, shape, F32))

        bufX = sbx("bufX", [128, NC8, T])
        bufY = sbx("bufY", [128, NC8, T])
        AR = 16384
        arena = sbx("arena", [128, AR])

        pos = [0]

        def aview(nelem, shape):
            off = pos[0]
            assert off + nelem <= AR, (off, nelem)
            pos[0] += nelem
            ap = arena.ap()[:, off:off + nelem]
            if len(shape) == 2:
                assert shape[1] == nelem
                return ap
            return ap.rearrange("p (a b) -> p a b", a=shape[1], b=shape[2])

        def aresset(v):
            pos[0] = v

        ones2 = sbx("ones2_sb", [2, 128])
        onesc = sbx("onesc_sb", [128, 2])
        e2c = sbx("e2_sb", [128, 2])
        ident = sbx("ident_sb", [128, 128])
        rw = sbx("rw_sb", [128, 8, 8])
        row_m = sbx("row_m", [2, 512])
        row_r = sbx("row_r", [2, 512])
        row_t = sbx("row_t", [2, 512])
        m8 = sbx("m8_sb", [128, 8])
        es8 = sbx("es8_sb", [128, 8])
        msk8 = sbx("msk8_sb", [128, 8])
        ncol = sbx("ncol_sb", [128, 1])
        rcol = sbx("rcol_sb", [128, 1])
        tw_sb = sbx("tw_sb", [128, NTC, 8])
        twTt = sbx("twTt_sb", [8, 128])
        sq = sbx("sq", [128, 2, 512])
        mneg = sbx("mneg", [128, 1])
        s_r2 = sbx("s_r2", [2, 1])
        s_r1 = sbx("s_r1", [1, 4])
        s_m1 = sbx("s_m1", [1, 2])
        qn2 = sbx("qn2", [2, 512])
        twst = sbx("twst", [2, 2, 512])
        onesHIt = sbx("onesHI", [66, 128])
        onesHI = onesHIt.ap()[64:66, :]
        s_mm = sbx("s_mm", [1, 1])
        s_m2b = sbx("s_m2b", [2, 2])
        arin_sb = sbx("arin_sb", [1, 1])

        TH = T // 2
        hts = aview(NFC * TH, [128, NFC, TH])
        twbc = aview(TH, [128, TH])
        wgs = aview(2 * 8 * 128, [128, 2, 1024])
        wus = aview(2 * 8 * 128, [128, 2, 1024])
        wds = aview(NFC * 1024, [128, NFC, 1024])
        sgh = aview(512, [128, 512])
        uth = aview(512, [128, 512])
        twT2v = aview(2048, [128, 2048])
        ffn_top = pos[0]

        twT2 = twT2v[0:16, :]
        aresset(0)
        wkv_sl = aview(2 * 8 * 512, [128, 2, 8 * 512]).rearrange(
            "p a (b c) -> p a b c", b=8, c=512)
        wqs = aview(2 * 1024, [128, 2, 1024])
        kvt = aview(2 * 528, [128, 2, 528])
        kn8 = aview(8, [128, 8])
        knmax = aview(16, [128, 16])
        knred = aview(1, [128, 1])
        qst = aview(2 * 512, [128, 2, 512])
        qsq = aview(512, [128, 512])
        rkcf = aview(1024, [128, 1024])
        rkc = rkcf.rearrange("p (a b) -> p a b", a=16, b=64)
        rksf = aview(1024, [128, 1024])
        rks = rksf.rearrange("p (a b) -> p a b", a=16, b=64)
        qcs = aview(512, [128, 512])
        qss = aview(512, [128, 512])
        ksq = qcs
        qkv_top = pos[0]
        assert max(ffn_top, qkv_top) <= AR

        aresset(0)
        kTp = aview(NKC * 128, [128, NKC, 128])
        vaug = aview(NKC * 2 * VSLOT, [128, NKC, 2 * VSLOT])
        qtile = aview(2 * 512, [128, 2, 512])
        kchunk = aview(2 * 128, [128, 2, 128])
        e_sb = aview(4 * 512, [128, 4, 512])
        rb_v = aview(2 * 512, [128, 2, 512])
        rb_sb = rb_v[0:64, :, :]
        rb2 = rb_v[64:66, 1, :]
        osh = aview(512, [128, 512])
        osh64 = osh[0:64, :]
        att_top = pos[0]
        assert att_top <= AR

        aresset(0)
        wos = aview(2 * 1024, [128, 2, 1024])
        aresset(0)

        pb = [psb(f"pb{i}", [128, 512]) for i in range(8)]

        Cm = lambda name: Ctr(ctx, nc, name)
        ld_x = Cm("ld_x"); ld_c = Cm("ld_c")
        f_ve = Cm("f_ve"); f_se = Cm("f_se")
        ln_sq = Cm("ln_sq"); ln_sqf = Cm("ln_sqf")
        ln_st = Cm("ln_st"); ln_rw = Cm("ln_rw"); ln_bc = Cm("ln_bc"); ln_ap = Cm("ln_ap")
        r_s = Cm("r_s"); r_a = Cm("r_a"); r_e = Cm("r_e"); r_f = Cm("r_f")
        r_t = Cm("r_t"); r_te = Cm("r_te"); r_d = Cm("r_d")
        m_w = Cm("m_w"); m_wf = Cm("m_wf")
        m_tb = Cm("m_tb"); m_tbe = Cm("m_tbe")
        m_gu = Cm("m_gu"); m_si = Cm("m_si"); m_ut = Cm("m_ut"); m_ht = Cm("m_ht")
        m_dn = Cm("m_dn"); m_xu = Cm("m_xu")
        c_ts = Cm("c_ts")
        k_w = Cm("k_w"); k_wf = Cm("k_wf"); k_mm = Cm("k_mm")
        k_ev = Cm("k_ev"); k_st = Cm("k_st")
        q_w = Cm("q_w"); q_wf = Cm("q_wf"); q_mm = Cm("q_mm")
        q_ev = Cm("q_ev"); q_st = Cm("q_st")
        qn_s = Cm("qn_s"); qn_m = Cm("qn_m"); qn_x = Cm("qn_x")
        kn_r = Cm("kn_r"); ms = Cm("ms"); msd = Cm("msd"); cc = Cm("cc")
        q_rw = Cm("q_rw"); q_rm = Cm("q_rm"); q_cl = Cm("q_cl"); rp_k = Cm("rp_k")
        a_kw = Cm("a_kw"); a_vw = Cm("a_vw"); a_qw = Cm("a_qw")
        a_tr = Cm("a_tr"); a_te = Cm("a_te")
        a_s = Cm("a_s"); a_e = Cm("a_e"); a_ef = Cm("a_ef"); a_pv = Cm("a_pv")
        a_rp = Cm("a_rp"); a_rb = Cm("a_rb"); a_rbe = Cm("a_rbe")
        a_ow = Cm("a_ow"); a_sh = Cm("a_sh")
        o_w = Cm("o_w"); o_wf = Cm("o_wf"); o_mm = Cm("o_mm"); o_xu = Cm("o_xu")
        st_out = Cm("st_out")

        def fence_v(v, inst):
            f_ve.inc(inst)
            f_ve.wait(v)

        def fence_s(s, inst):
            f_se.inc(inst)
            f_se.wait(s)

        with nc.Block() as b:
            @b.sync
            def _(sync):
                for kc in range(NC8):
                    ld_x.inc(sync.dma_start(out=bufX[:, kc, :],
                                            in_=xT_d[kc * 128:(kc + 1) * 128, :]), 16)

            @b.scalar
            def _(sc):
                ld_c.inc(sc.dma_start(out=ones2[:], in_=ones2_d), 16)
                ld_c.inc(sc.dma_start(out=onesHI, in_=ones2_d), 16)
                ld_c.inc(sc.dma_start(out=onesc[:], in_=onesc_d), 16)
                ld_c.inc(sc.dma_start(out=e2c[:], in_=e2_d), 16)
                ld_c.inc(sc.dma_start(out=ident[:], in_=id_d), 16)
                ld_c.inc(sc.dma_start(out=rw[:, :, :].rearrange("p a b -> p (a b)"),
                                      in_=rw_d.rearrange("p a b -> p (a b)")), 16)

        def emit_ln(xbuf, ybuf, need_ldx=False):
            b_sq, b_sqf = ln_sq.v, ln_sqf.v
            b_st, b_rw, b_bc, b_ap = ln_st.v, ln_rw.v, ln_bc.v, ln_ap.v
            with nc.Block() as blk:
                @blk.vector
                def _(v):
                    if need_ldx:
                        ld_x.wait(v)
                    for g in range(NG):
                        gs = slice(g * 512, (g + 1) * 512)
                        for kc in range(NC8):
                            i = g * NC8 + kc
                            if i >= 2:
                                v.wait_ge(ln_sqf.h, b_sqf + i - 1)
                            v.tensor_mul(sq[:, (b_sq + i) % 2, :], xbuf[:, kc, gs],
                                         xbuf[:, kc, gs]).then_inc(ln_sq.h, 1)
                        v.wait_ge(ln_sqf.h, b_sqf + NC8 * (g + 1))
                        v.tensor_scalar_mul(row_m[:], pb[0].ap()[0:2, :], 1.0 / D)
                        v.tensor_scalar_mul(row_t[:], pb[1].ap()[0:2, :], 1.0 / D)
                        v.tensor_mul(row_r[:], row_m[:], row_m[:])
                        v.tensor_sub(row_t[:], row_t[:], row_r[:]).then_inc(ln_rw.h, 1)
                        v.tensor_scalar_add(row_t[:], row_t[:], float(EPS)).then_inc(ln_rw.h, 1)
                        v.wait_ge(ln_rw.h, b_rw + g * 4 + 3)
                        v.reciprocal(row_r[:], row_r[:]).then_inc(ln_rw.h, 1)
                        v.wait_ge(ln_bc.h, b_bc + g + 1)
                        for kc in range(NC8):
                            v.tensor_sub(ybuf[:, kc, gs], xbuf[:, kc, gs], pb[2][:])
                            ii = v.tensor_mul(ybuf[:, kc, gs], ybuf[:, kc, gs], pb[3][:])
                        ii.then_inc(ln_ap.h, 1)
                    ln_sq.v += NG * NC8
                    ln_rw.v += NG * 4
                    ln_ap.v += NG

                @blk.scalar
                def _(s):
                    for g in range(NG):
                        s.wait_ge(ln_rw.h, b_rw + g * 4 + 2)
                        s.activation(out=row_r[:], in_=row_t[:],
                                     func=AF.Sqrt).then_inc(ln_rw.h, 1)

                @blk.tensor
                def _(t):
                    if need_ldx:
                        ld_x.wait(t)
                        ld_c.wait(t)
                    for g in range(NG):
                        gs = slice(g * 512, (g + 1) * 512)
                        if b_ap + g > 0:
                            t.wait_ge(ln_ap.h, b_ap + g)
                        for kc in range(NC8):
                            t.matmul(pb[0].ap()[0:2, :], onesc[:], xbuf[:, kc, gs],
                                     start=(kc == 0), stop=(kc == NC8 - 1))
                        for kc in range(NC8):
                            i = g * NC8 + kc
                            t.wait_ge(ln_sq.h, b_sq + i + 1)
                            t.matmul(pb[1].ap()[0:2, :], onesc[:],
                                     sq[:, (b_sq + i) % 2, :],
                                     start=(kc == 0),
                                     stop=(kc == NC8 - 1)).then_inc(ln_sqf.h, 1)
                        t.wait_ge(ln_rw.h, b_rw + g * 4 + 4)
                        t.matmul(pb[2][:], ones2[:], row_m[:], start=True, stop=True)
                        t.matmul(pb[3][:], ones2[:], row_r[:],
                                 start=True, stop=True).then_inc(ln_bc.h, 1)
                    ln_sqf.v += NG * NC8
                    ln_bc.v += NG

        def emit_router(ybuf):
            b_s, b_a, b_e, b_f = r_s.v, r_a.v, r_e.v, r_f.v
            b_t, b_te, b_d = r_t.v, r_te.v, r_d.v
            with nc.Block() as blk:
                @blk.tensor
                def _(t):
                    for tc in range(NTC):
                        ts = slice(tc * 128, (tc + 1) * 128)
                        if tc > 0:
                            t.wait_ge(r_a.h, b_a + 2 * (tc - 1) + 1)
                        for kc in range(NC8):
                            mm = t.matmul(pb[4].ap()[:, 0:8], ybuf[:, kc, ts],
                                          rw[:, kc, :], start=(kc == 0),
                                          stop=(kc == NC8 - 1))
                        mm.then_inc(r_s.h, 1)
                        t.wait_ge(r_f.h, b_f + tc + 1)
                        if tc > 0:
                            t.wait_ge(r_te.h, b_te + tc)
                        t.transpose(pb[5].ap()[0:8, 0:128], tw_sb[:, tc, :],
                                    ident[:]).then_inc(r_t.h, 1)
                    r_s.v += NTC
                    r_t.v += NTC

                @blk.vector
                def _(v):
                    for tc in range(NTC):
                        v.wait_ge(r_s.h, b_s + tc + 1)
                        v.tensor_copy(es8[:], pb[4].ap()[:, 0:8]).then_inc(r_a.h, 1)
                        v.wait_ge(r_a.h, b_a + 2 * tc + 1)
                        fence_v(v, v.max(out=m8[:], in_=es8[:]))
                        fence_v(v, v.tensor_scalar_mul(ncol[:], m8[:, 0:1], -1.0))
                        v.tensor_tensor(msk8[:], es8[:],
                                        m8[:, 5:6].broadcast_to([128, 8]),
                                        ALU.is_ge).then_inc(r_a.h, 1)
                        v.wait_ge(r_a.h, b_a + 2 * tc + 2)
                        v.wait_ge(r_e.h, b_e + tc + 1)
                        fence_v(v, v.reduce_sum(rcol[:], es8[:], AX.X))
                        fence_v(v, v.reciprocal(rcol[:], rcol[:]))
                        fence_v(v, v.tensor_tensor(es8[:], es8[:],
                                                   rcol[:, 0:1].broadcast_to([128, 8]),
                                                   ALU.mult))
                        v.tensor_mul(tw_sb[:, tc, :], es8[:],
                                     msk8[:]).then_inc(r_f.h, 1)
                        v.wait_ge(r_t.h, b_t + tc + 1)
                        if tc > 0:
                            v.wait_ge(r_d.h, b_d + 256 * tc)
                        v.tensor_copy(twTt[:], pb[5].ap()[0:8, 0:128]).then_inc(r_te.h, 1)
                    r_a.v += 2 * NTC
                    r_f.v += NTC
                    r_te.v += NTC

                @blk.scalar
                def _(s):
                    for tc in range(NTC):
                        s.wait_ge(r_a.h, b_a + 2 * tc + 2)
                        s.activation(out=es8[:], in_=es8[:], func=AF.Exp,
                                     bias=ncol[:, 0:1], scale=1.0).then_inc(r_e.h, 1)
                    r_e.v += NTC

                @blk.sync
                def _(sy):
                    for tc in range(NTC):
                        ts = slice(tc * 128, (tc + 1) * 128)
                        sy.wait_ge(r_te.h, b_te + tc + 1)
                        for e in range(8):
                            sy.dma_start(out=twT2[2 * e:2 * e + 1, ts],
                                         in_=twTt[e:e + 1, :]).then_inc(r_d.h, 16)
                            sy.dma_start(out=twT2[2 * e + 1:2 * e + 2, ts],
                                         in_=twTt[e:e + 1, :]).then_inc(r_d.h, 16)
                    r_d.v += 256 * NTC

        def emit_ffn(xbuf, ybuf):
            emit_router(ybuf)
            b_mw, b_wf = m_w.v, m_wf.v
            b_gu, b_si, b_ut, b_ht = m_gu.v, m_si.v, m_ut.v, m_ht.v
            b_dn, b_xu = m_dn.v, m_xu.v
            b_tb, b_tbe, b_ts = m_tb.v, m_tbe.v, c_ts.v
            b_rd = r_d.v

            def gu_done_val(xp, half, fc):
                return b_mw + xp * 272 + 16 + half * 128 + (fc + 1) * 32

            def wd_done_val(xp):
                return b_mw + xp * 272 + 16

            def it_idx(xp, half, fc, g2):
                return ((xp * 2 + half) * NFC + fc) * 2 + g2

            def dn_idx(xp, half, g2, do):
                return ((xp * 2 + half) * 2 + g2) * NC8 + do

            def twi(xp, half):
                return xp * 2 + half

            NIT = NXP * 2 * NFC * 2
            NDN = NXP * 2 * 2 * NC8

            with nc.Block() as blk:
                @blk.sync
                def _(sync):
                    sync.wait_ge(r_d.h, b_rd)
                    for xp in range(NXP):
                        sync.wait_ge(m_dn.h, b_dn + xp * 32)
                        sync.dma_start(
                            out=wds[:, :, :].rearrange("p a b -> p (a b)"),
                            in_=wd_d[xp].rearrange("p a b -> p (a b)")).then_inc(m_w.h, 16)
                        for half in range(2):
                            if xp < NE:
                                j = twi(xp, half)
                                if j > 0:
                                    sync.wait_ge(m_tb.h, b_tb + 2 * j)
                                for g2 in range(2):
                                    gs = slice(half * TH + g2 * 512,
                                               half * TH + (g2 + 1) * 512)
                                    sync.dma_start(
                                        out=twst[:, g2, :],
                                        in_=twT2[2 * xp:2 * xp + 2, gs]).then_inc(c_ts.h, 16)
                            for fc in range(NFC):
                                s_i = (xp * 2 + half) * NFC + fc
                                if s_i >= 2:
                                    sync.wait_ge(m_gu.h, b_gu + (s_i - 1) * 2)
                                sync.dma_start(
                                    out=wgs[:, s_i % 2, :],
                                    in_=wg_d[xp, fc].rearrange("p a b -> p (a b)")).then_inc(m_w.h, 16)
                                sync.dma_start(
                                    out=wus[:, s_i % 2, :],
                                    in_=wu_d[xp, fc].rearrange("p a b -> p (a b)")).then_inc(m_w.h, 16)
                    m_w.v += NXP * 272
                    c_ts.v += NE * 2 * 32

                @blk.tensor
                def _(t):
                    nmm_tb = 0
                    for xp in range(NXP):
                        for half in range(2):
                            if xp < NE:
                                j = twi(xp, half)
                                t.wait_ge(c_ts.h, b_ts + 32 * (j + 1))
                                for g2 in range(2):
                                    if nmm_tb >= 2:
                                        t.wait_ge(m_tbe.h, b_tbe + nmm_tb - 1)
                                    t.matmul(pb[6 + nmm_tb % 2][:], ones2[:],
                                             twst[:, g2, :], start=True,
                                             stop=True).then_inc(m_tb.h, 1)
                                    nmm_tb += 1
                            for fc in range(NFC):
                                s_i = (xp * 2 + half) * NFC + fc
                                slot = s_i % 2
                                for g2 in range(2):
                                    gs = slice(half * TH + g2 * 512,
                                               half * TH + (g2 + 1) * 512)
                                    i = it_idx(xp, half, fc, g2)
                                    t.wait_ge(m_w.h, gu_done_val(xp, half, fc))
                                    if i > 0:
                                        t.wait_ge(m_si.h, b_si + i)
                                        t.wait_ge(m_ut.h, b_ut + i)
                                    if serial and i > 0:
                                        t.wait_ge(m_ht.h, b_ht + i)
                                        t.wait_ge(m_xu.h, b_xu + m_xu.v - b_xu)
                                    for kc in range(NC8):
                                        t.matmul(pb[0][:],
                                                 wgs[:, slot, kc * 128:(kc + 1) * 128],
                                                 ybuf[:, kc, gs], start=(kc == 0),
                                                 stop=(kc == NC8 - 1))
                                    for kc in range(NC8):
                                        mm = t.matmul(pb[1][:],
                                                      wus[:, slot, kc * 128:(kc + 1) * 128],
                                                      ybuf[:, kc, gs], start=(kc == 0),
                                                      stop=(kc == NC8 - 1))
                                    mm.then_inc(m_gu.h, 1)
                            t.wait_ge(m_ht.h, b_ht + (xp * 2 + half + 1) * NFC * 2)
                            t.wait_ge(m_w.h, wd_done_val(xp))
                            for g2 in range(2):
                                hs = slice(g2 * 512, (g2 + 1) * 512)
                                for do in range(NC8):
                                    j = dn_idx(xp, half, g2, do)
                                    if j > 0:
                                        t.wait_ge(m_xu.h, b_xu + j)
                                    if serial and j > 0:
                                        t.wait_ge(m_ht.h, m_ht.v)
                                    for fc in range(NFC):
                                        mm = t.matmul(pb[2][:],
                                                      wds[:, fc, do * 128:(do + 1) * 128],
                                                      hts[:, fc, hs], start=(fc == 0),
                                                      stop=(fc == NFC - 1))
                                    mm.then_inc(m_dn.h, 1)
                    m_tb.v += NE * 2 * 2
                    m_gu.v += NIT
                    m_dn.v += NDN

                @blk.scalar
                def _(s):
                    for xp in range(NXP):
                        for half in range(2):
                            for fc in range(NFC):
                                for g2 in range(2):
                                    i = it_idx(xp, half, fc, g2)
                                    s.wait_ge(m_gu.h, b_gu + i + 1)
                                    if i > 0:
                                        s.wait_ge(m_ht.h, b_ht + i)
                                    s.activation(out=sgh[:], in_=pb[0][:],
                                                 func=AF.Silu).then_inc(m_si.h, 1)
                    m_si.v += NIT

                @blk.vector
                def _(v):
                    nmm_tb = 0
                    for xp in range(NXP):
                        for half in range(2):
                            if xp < NE:
                                for g2 in range(2):
                                    v.wait_ge(m_tb.h, b_tb + nmm_tb + 1)
                                    v.tensor_copy(twbc[:, g2 * 512:(g2 + 1) * 512],
                                                  pb[6 + nmm_tb % 2][:]).then_inc(m_tbe.h, 1)
                                    nmm_tb += 1
                            for fc in range(NFC):
                                for g2 in range(2):
                                    hs = slice(g2 * 512, (g2 + 1) * 512)
                                    i = it_idx(xp, half, fc, g2)
                                    v.wait_ge(m_gu.h, b_gu + i + 1)
                                    if xp < NE:
                                        v.tensor_tensor(uth[:], pb[1][:], twbc[:, hs],
                                                        ALU.mult).then_inc(m_ut.h, 1)
                                    else:
                                        v.tensor_copy(uth[:], pb[1][:]).then_inc(m_ut.h, 1)
                                    v.wait_ge(m_si.h, b_si + i + 1)
                                    v.tensor_mul(hts[:, fc, hs], sgh[:],
                                                 uth[:]).then_inc(m_ht.h, 1)
                            for g2 in range(2):
                                gs = slice(half * TH + g2 * 512,
                                           half * TH + (g2 + 1) * 512)
                                for do in range(NC8):
                                    j = dn_idx(xp, half, g2, do)
                                    v.wait_ge(m_dn.h, b_dn + j + 1)
                                    v.scalar_tensor_tensor(
                                        xbuf[:, do, gs], pb[2][:], 0.5, xbuf[:, do, gs],
                                        ALU.mult, ALU.add).then_inc(m_xu.h, 1)
                    m_tbe.v += NE * 2 * 2
                    m_ut.v += NIT
                    m_ht.v += NIT
                    m_xu.v += NDN

        def emit_qkv(ybuf):
            b_kw, b_kwf, b_kmm = k_w.v, k_wf.v, k_mm.v
            b_kev, b_kst = k_ev.v, k_st.v
            b_qw, b_qwf, b_qmm = q_w.v, q_wf.v, q_mm.v
            b_qev, b_qst = q_ev.v, q_st.v
            b_qns, b_qnm, b_qnx = qn_s.v, qn_m.v, qn_x.v
            b_knr, b_ms, b_msd, b_cc = kn_r.v, ms.v, msd.v, cc.v
            b_qrw, b_qrm, b_qcl, b_rpk = q_rw.v, q_rm.v, q_cl.v, rp_k.v
            NQT = NC8 * NG
            sq0 = sq.ap()[:, 0, :]
            sq1 = sq.ap()[:, 1, :]

            with nc.Block() as blk:

                @blk.tensor
                def _(t):
                    for s in range(4):
                        t.wait_ge(k_w.h, b_kw + (s + 1) * 128)
                        for tc in range(NTC):
                            i = s * NTC + tc
                            ts = slice(tc * 128, (tc + 1) * 128)
                            if i >= 2:
                                t.wait_ge(k_ev.h, b_kev + i - 1)
                            for kc in range(NC8):
                                mm = t.matmul(pb[i % 2][:], ybuf[:, kc, ts],
                                              wkv_sl[:, s % 2, kc, :],
                                              start=(kc == 0), stop=(kc == NC8 - 1))
                            mm.then_inc(k_mm.h, 1)
                    k_mm.v += 4 * NTC
                    for ic in range(NC8):
                        t.wait_ge(q_w.h, b_qw + (ic + 1) * 16)
                        for g in range(NG):
                            gs = slice(g * 512, (g + 1) * 512)
                            i = ic * NG + g
                            if i >= 1:
                                t.wait_ge(q_st.h, b_qst + 16 * i)
                            if i >= 2:
                                t.wait_ge(q_rw.h, b_qrw + i - 1)
                            for kc in range(NC8):
                                mm = t.matmul(pb[2 + i % 2][:],
                                              wqs[:, ic % 2, kc * 128:(kc + 1) * 128],
                                              ybuf[:, kc, gs],
                                              start=(kc == 0), stop=(kc == NC8 - 1))
                            mm.then_inc(q_mm.h, 1)
                            j = i - 1
                            if j >= 0:
                                t.wait_ge(qn_s.h, b_qns + j + 1)
                                if j >= 1:
                                    t.wait_ge(qn_x.h, b_qnx + j)
                                t.matmul(pb[4].ap()[0:2, :], e2c[:], qsq[:],
                                         start=True, stop=True).then_inc(qn_m.h, 1)
                    j = NQT - 1
                    t.wait_ge(qn_s.h, b_qns + j + 1)
                    t.wait_ge(qn_x.h, b_qnx + j)
                    t.matmul(pb[4].ap()[0:2, :], e2c[:], qsq[:],
                             start=True, stop=True).then_inc(qn_m.h, 1)
                    q_mm.v += NQT
                    qn_m.v += NQT
                    t.wait_ge(kn_r.h, b_knr + 1)
                    t.wait_ge(ms.h, b_ms + 1)
                    t.transpose(pb[5].ap()[0:1, 0:128], knred[:],
                                ident[:]).then_inc(ms.h, 1)
                    t.wait_ge(msd.h, b_msd + 128)
                    t.matmul(pb[5].ap()[:, 0:2], ones2[:], s_m2b[:],
                             start=True, stop=True).then_inc(ms.h, 1)
                    ms.v += 2

                @blk.vector
                def _(v):
                    v.wait_ge(rp_k.h, b_rpk + 32)
                    for s in range(4):
                        if s == 2:
                            v.wait_ge(k_st.h, b_kst + 16 * 2 * NTC)
                            for sl in range(2):
                                v.memset(kvt[:, sl, 0:528].rearrange(
                                    "p (a b) -> p a b", a=8, b=66)[:, :, 64:66], 1.0)
                        for tc in range(NTC):
                            i = s * NTC + tc
                            v.wait_ge(k_mm.h, b_kmm + i + 1)
                            if i >= 2:
                                v.wait_ge(k_st.h, b_kst + 16 * (i - 1))
                            if s < 2:
                                ps = pb[i % 2][:]
                                for h in range(8):
                                    hb = h * 64
                                    v.tensor_tensor(sq0[:, hb:hb + 32],
                                                    ps[:, hb + 32:hb + 64],
                                                    rks[:, tc, 0:32], ALU.mult)
                                    v.tensor_tensor(sq0[:, hb + 32:hb + 64],
                                                    ps[:, hb:hb + 32],
                                                    rks[:, tc, 32:64], ALU.mult)
                                    ivh = v.tensor_tensor(kvt[:, i % 2, hb:hb + 64],
                                                          ps[:, hb:hb + 64],
                                                          rkc[:, tc, :], ALU.mult)
                                fence_v(v, ivh)
                                iv = v.tensor_tensor(kvt[:, i % 2, 0:512],
                                                     kvt[:, i % 2, 0:512],
                                                     sq0[:], ALU.add)
                            else:
                                iv = v.tensor_copy(
                                    kvt[:, i % 2, 0:528].rearrange(
                                        "p (a b) -> p a b", a=8, b=66)[:, :, 0:64],
                                    pb[i % 2][:].rearrange("p (a b) -> p a b", a=8, b=64))
                            iv.then_inc(k_ev.h, 1)
                            if s < 2:
                                v.tensor_mul(ksq[:], kvt[:, i % 2, 0:512],
                                             kvt[:, i % 2, 0:512])
                                fence_v(v, v.reduce_sum(
                                    kn8[:], ksq[:].rearrange("p (a b) -> p a b", a=8, b=64),
                                    AX.X))
                                c0 = s * 8
                                if tc == 0:
                                    fence_v(v, v.tensor_copy(knmax[:, c0:c0 + 8], kn8[:]))
                                else:
                                    fence_v(v, v.tensor_max(knmax[:, c0:c0 + 8],
                                                            knmax[:, c0:c0 + 8], kn8[:]))
                    k_ev.v += 4 * NTC
                    fence_v(v, v.reduce_max(knred[:], knmax[:], AX.X))
                    v.engine_nop().then_inc(kn_r.h, 1)
                    kn_r.v += 1
                    for i in range(NQT):
                        v.wait_ge(q_mm.h, b_qmm + i + 1)
                        if i >= 2:
                            v.wait_ge(q_st.h, b_qst + 16 * (i - 1))
                        v.tensor_copy(qst[:, i % 2, :], pb[2 + i % 2][:]).then_inc(q_rw.h, 1)
                        if i >= 1:
                            v.wait_ge(qn_m.h, b_qnm + i)
                        v.tensor_mul(qsq[:], qst[:, i % 2, :],
                                     qst[:, i % 2, :]).then_inc(qn_s.h, 1)
                        if i >= 1:
                            if i == 1:
                                iv = v.tensor_copy(qn2[:], pb[4].ap()[0:2, :])
                            else:
                                iv = v.tensor_max(qn2[:], qn2[:], pb[4].ap()[0:2, :])
                            iv.then_inc(qn_x.h, 1)
                        v.wait_ge(q_cl.h, b_qcl + 32 * (i + 1))
                        v.wait_ge(q_rm.h, b_qrm + 64 * (i + 1))
                        v.tensor_tensor(qst[:, i % 2, :], qst[:, i % 2, :],
                                        qcs[:], ALU.mult)
                        v.tensor_tensor(sq1[:], sq1[:], qss[:], ALU.mult)
                        v.tensor_tensor(qst[:, i % 2, :], qst[:, i % 2, :],
                                        sq1[:], ALU.add).then_inc(q_ev.h, 1)
                    v.wait_ge(qn_m.h, b_qnm + NQT)
                    v.tensor_max(qn2[:], qn2[:], pb[4].ap()[0:2, :]).then_inc(qn_x.h, 1)
                    q_ev.v += NQT
                    q_rw.v += NQT
                    qn_s.v += NQT
                    qn_x.v += NQT
                    fence_v(v, v.reduce_max(s_r2[:], qn2[:], AX.X))
                    v.engine_nop().then_inc(ms.h, 1)
                    v.wait_ge(ms.h, b_ms + 2)
                    fence_v(v, v.reduce_max(s_mm[:], pb[5].ap()[0:1, 0:128], AX.X))
                    v.engine_nop().then_inc(ms.h, 1)
                    v.wait_ge(msd.h, b_msd + 64)
                    fence_v(v, v.reduce_max(s_mm[:], s_r1[0:1, 0:2], AX.X))
                    fence_v(v, v.tensor_mul(s_mm[:], s_mm[:], s_r1[0:1, 2:3]))
                    v.engine_nop().then_inc(ms.h, 1)
                    v.wait_ge(ms.h, b_ms + 5)
                    fence_v(v, v.tensor_scalar_mul(s_mm[:], s_mm[:], -1.0001 / 8.0))
                    v.engine_nop().then_inc(ms.h, 1)
                    v.wait_ge(ms.h, b_ms + 7)
                    fence_v(v, v.tensor_copy(mneg[:], pb[5].ap()[:, 0:1]))
                    v.engine_nop().then_inc(ms.h, 1)
                    ms.v += 6

                @blk.scalar
                def _(s_):
                    s_.dma_start(out=rkcf, in_=ropeKc_d).then_inc(rp_k.h, 16)
                    s_.dma_start(out=rksf, in_=ropeKs_d).then_inc(rp_k.h, 16)
                    rp_k.v += 32
                    for i in range(NQT):
                        g = i % NG
                        if i == 0:
                            s_.wait_ge(k_ev.h, b_kev + 4 * NTC)
                        s_.wait_ge(q_ev.h, b_qev + i)
                        s_.dma_start(out=qcs[:], in_=ropeQc_d[g]).then_inc(q_cl.h, 16)
                        s_.dma_start(out=qss[:], in_=ropeQs_d[g]).then_inc(q_cl.h, 16)
                        s_.wait_ge(q_rw.h, b_qrw + i + 1)
                        s_.dma_start(out=sq1[0:32, :],
                                     in_=qst[32:64, i % 2, :]).then_inc(q_rm.h, 16)
                        s_.dma_start(out=sq1[32:64, :],
                                     in_=qst[0:32, i % 2, :]).then_inc(q_rm.h, 16)
                        s_.dma_start(out=sq1[64:96, :],
                                     in_=qst[96:128, i % 2, :]).then_inc(q_rm.h, 16)
                        s_.dma_start(out=sq1[96:128, :],
                                     in_=qst[64:96, i % 2, :]).then_inc(q_rm.h, 16)
                    q_cl.v += NQT * 32
                    q_rm.v += NQT * 64
                    s_.wait_ge(ms.h, b_ms + 4)
                    s_.activation(out=s_mm[:], in_=s_mm[:],
                                  func=AF.Sqrt).then_inc(ms.h, 1)
                    ms.v += 1

                @blk.sync
                def _(sy):
                    def kv_store(s, tc):
                        i = s * NTC + tc
                        sy.wait_ge(k_ev.h, b_kev + i + 1)
                        rows = slice(tc * 128, (tc + 1) * 128)
                        if s < 2:
                            sy.dma_start(
                                out=kvin_d[rows, (s % 2) * 512:(s % 2) * 512 + 512],
                                in_=kvt[:, i % 2, 0:512]).then_inc(k_st.h, 16)
                        else:
                            sy.dma_start(
                                out=kvin_d[rows, D + (s % 2) * 528:D + (s % 2) * 528 + 528],
                                in_=kvt[:, i % 2, 0:528]).then_inc(k_st.h, 16)

                    def q_store(i):
                        sy.wait_ge(q_ev.h, b_qev + i + 1)
                        ic, gg = divmod(i, NG)
                        sy.dma_start(
                            out=qT_d[ic * 128:(ic + 1) * 128, gg * 512:(gg + 1) * 512],
                            in_=qst[:, i % 2, :]).then_inc(q_st.h, 16)

                    for s in range(4):
                        if s >= 2:
                            for tc in range(NTC):
                                kv_store(s - 2, tc)
                        wsrc = wk_d if s < 2 else wv_d
                        ih = s % 2
                        for kc in range(NC8):
                            sy.dma_start(
                                out=wkv_sl[:, s % 2, kc, :],
                                in_=wsrc[kc, :, ih * 512:(ih + 1) * 512]).then_inc(k_w.h, 16)
                    for s in (2, 3):
                        for tc in range(NTC):
                            kv_store(s, tc)
                    k_w.v += 4 * 8 * 16
                    k_st.v += 4 * NTC * 16
                    for ic in range(NC8):
                        if ic >= 2:
                            for g in range(NG):
                                q_store((ic - 2) * NG + g)
                        sy.dma_start(out=wqs[:, ic % 2, :],
                                     in_=wq_d[ic].rearrange("p a b -> p (a b)")).then_inc(q_w.h, 16)
                    for i in range((NC8 - 2) * NG, NQT):
                        q_store(i)
                    q_w.v += NC8 * 16
                    q_st.v += NQT * 16
                    sy.wait_ge(ms.h, b_ms + 3)
                    sy.dma_start(out=arin_d, in_=s_mm[:]).then_inc(msd.h, 16)
                    msd.v += 16
                    sy.wait_ge(cc.h, b_cc + 2)
                    sy.dma_start(out=s_r1[0:1, 2:3], in_=arout_d).then_inc(msd.h, 16)
                    sy.dma_start(out=s_r1[0:1, 0:1], in_=s_r2[0:1, 0:1]).then_inc(msd.h, 16)
                    sy.dma_start(out=s_r1[0:1, 1:2], in_=s_r2[1:2, 0:1]).then_inc(msd.h, 16)
                    msd.v += 48
                    sy.wait_ge(ms.h, b_ms + 6)
                    for qd in range(4):
                        sy.dma_start(out=s_m2b[qd // 2:qd // 2 + 1, qd % 2:qd % 2 + 1],
                                     in_=s_mm[:]).then_inc(msd.h, 16)
                    msd.v += 64

                @blk.gpsimd
                def _(g):
                    g.wait_ge(k_st.h, b_kst + 4 * NTC * 16)
                    g.collective_compute(
                        "AllGather", ALU.bypass, replica_groups=groups,
                        ins=[kvin_d], outs=[kvout_d]).then_inc(cc.h, 1)
                    cc.v += 1
                    g.wait_ge(msd.h, b_msd + 16)
                    g.collective_compute(
                        "AllReduce", ALU.max, replica_groups=groups,
                        ins=[arin_d], outs=[arout_d]).then_inc(cc.h, 1)
                    cc.v += 1

        def emit_attn(hp):
            b_kw_, b_vw, b_qw_ = a_kw.v, a_vw.v, a_qw.v
            b_tr, b_te = a_tr.v, a_te.v
            b_s, b_e, b_pv = a_s.v, a_e.v, a_pv.v
            b_rp, b_rb, b_rbe = a_rp.v, a_rb.v, a_rbe.v
            b_ow, b_sh = a_ow.v, a_sh.v

            def sdone(qg):
                return b_s + (qg + 1) * 2 * NKC

            with nc.Block() as blk:
                @blk.sync
                def _(sync):
                    if hp == 0:
                        cc.wait(sync)
                        q_st.wait(sync)
                    for kc2 in range(NKC):
                        if kc2 >= 2:
                            sync.wait_ge(a_tr.h, b_tr + kc2 - 1)
                        sync.dma_start(
                            out=kchunk[:, kc2 % 2, :],
                            in_=kvout_d[kc2 * 128:(kc2 + 1) * 128,
                                        hp * 128:(hp + 1) * 128]).then_inc(a_kw.h, 16)
                    a_kw.v += NKC * 16
                    for kc2 in range(NKC):
                        sync.dma_start(
                            out=vaug[:, kc2, :],
                            in_=kvout_d[kc2 * 128:(kc2 + 1) * 128,
                                        D + hp * 132:D + (hp + 1) * 132]).then_inc(a_vw.h, 16)
                    a_vw.v += NKC * 16
                    for qg in range(NG):
                        if qg >= 2:
                            sync.wait_ge(a_s.h, sdone(qg - 1))
                        sync.dma_start(
                            out=qtile[:, qg % 2, :],
                            in_=qT_d[hp * 128:(hp + 1) * 128,
                                     qg * 512:(qg + 1) * 512]).then_inc(a_qw.h, 16)
                    a_qw.v += NG * 16

                @blk.tensor
                def _(t):
                    for kc2 in range(NKC):
                        t.wait_ge(a_kw.h, b_kw_ + (kc2 + 1) * 16)
                        if kc2 >= 1:
                            t.wait_ge(a_te.h, b_te + kc2)
                        t.transpose(pb[kc2 % 2][:, 0:128], kchunk[:, kc2 % 2, :],
                                    ident[:]).then_inc(a_tr.h, 1)
                    a_tr.v += NKC
                    t.wait_ge(a_te.h, b_te + NKC)
                    t.wait_ge(a_vw.h, b_vw + NKC * 16)
                    for qg in range(NG):
                        t.wait_ge(a_qw.h, b_qw_ + (qg + 1) * 16)
                        if qg > 0:
                            t.wait_ge(a_ow.h, b_ow + qg * 2)
                            t.wait_ge(a_rbe.h, b_rbe + qg * 2)
                        for kk in range(NKC):
                            gi = qg * NKC + kk
                            t.wait_ge(a_te.h, b_te + kk + 1)
                            for h in range(2):
                                sb_ = 2 + h + 2 * (kk % 2)
                                if serial and gi >= 1:
                                    t.wait_ge(a_e.h, b_e + gi * 2)
                                if gi >= 2:
                                    t.wait_ge(a_e.h, b_e + (gi - 2) * 2 + h + 1)
                                t.matmul(pb[sb_][:], kTp[h * 64:(h + 1) * 64, kk, :],
                                         qtile[h * 64:(h + 1) * 64, qg % 2, :],
                                         start=True, stop=True,
                                         tile_position=(h * 64, 0)).then_inc(a_s.h, 1)
                            for h in range(2):
                                t.wait_ge(a_e.h, b_e + gi * 2 + h + 1)
                                t.wait_ge(a_vw.h, b_vw + (kk + 1) * 16)
                                t.matmul(pb[6 + h].ap()[0:66, :],
                                         vaug[:, kk, h * 66:(h + 1) * 66],
                                         e_sb[:, (kk % 2) * 2 + h, :],
                                         start=(kk == 0),
                                         stop=(kk == NKC - 1)).then_inc(a_pv.h, 1)
                        for h in range(2):
                            t.wait_ge(a_rp.h, b_rp + qg * 2 + h + 1)
                            t.matmul(pb[2 + h].ap()[0:64, :],
                                     onesHIt.ap()[64:66, 0:64], rb2[:],
                                     start=True, stop=True).then_inc(a_rb.h, 1)
                    a_s.v += NG * NKC * 2
                    a_pv.v += NG * NKC * 2
                    a_rb.v += NG * 2

                @blk.scalar
                def _(s_):
                    for qg in range(NG):
                        if qg > 0:
                            gs_p = slice((qg - 1) * 512, qg * 512)
                            s_.wait_ge(a_ow.h, b_ow + qg * 2)
                            s_.dma_start(out=bufY[64:128, hp, gs_p],
                                         in_=osh64[:]).then_inc(a_sh.h, 16)
                        for kk in range(NKC):
                            for h in range(2):
                                i2 = (qg * NKC + kk) * 2 + h
                                s_.wait_ge(a_s.h, b_s + i2 + 1)
                                if serial and qg * NKC + kk >= 1:
                                    s_.wait_ge(a_pv.h, b_pv + (qg * NKC + kk) * 2)
                                if qg * NKC + kk >= 2:
                                    s_.wait_ge(a_pv.h, b_pv + (qg * NKC + kk - 2) * 2 + h + 1)
                                sb_ = 2 + h + 2 * (kk % 2)
                                s_.activation(out=e_sb[:, (kk % 2) * 2 + h, :],
                                              in_=pb[sb_][:], func=AF.Exp,
                                              bias=mneg[:, 0:1],
                                              scale=0.125).then_inc(a_e.h, 1)
                    gs_p = slice((NG - 1) * 512, NG * 512)
                    s_.wait_ge(a_ow.h, b_ow + NG * 2)
                    s_.dma_start(out=bufY[64:128, hp, gs_p],
                                 in_=osh64[:]).then_inc(a_sh.h, 16)
                    a_e.v += NG * NKC * 2
                    a_sh.v += NG * 16
                    s_.wait_ge(a_sh.h, a_sh.v)

                @blk.vector
                def _(v):
                    for kc2 in range(NKC):
                        v.wait_ge(a_tr.h, b_tr + kc2 + 1)
                        v.tensor_copy(kTp[:, kc2, :],
                                      pb[kc2 % 2][:, 0:128]).then_inc(a_te.h, 1)
                    a_te.v += NKC
                    for qg in range(NG):
                        gs = slice(qg * 512, (qg + 1) * 512)
                        v.wait_ge(a_pv.h, b_pv + (qg + 1) * 2 * NKC)
                        for h in range(2):
                            fence_v(v, v.reciprocal(rb2[:], pb[6 + h].ap()[64:66, :]))
                            v.engine_nop().then_inc(a_rp.h, 1)
                            v.wait_ge(a_rb.h, b_rb + qg * 2 + h + 1)
                            v.tensor_copy(rb_sb[:, h, :],
                                          pb[2 + h].ap()[0:64, :]).then_inc(a_rbe.h, 1)
                            if h == 0:
                                v.tensor_mul(bufY[0:64, hp, gs], pb[6].ap()[0:64, :],
                                             rb_sb[:, 0, :]).then_inc(a_ow.h, 1)
                            else:
                                if qg > 0:
                                    v.wait_ge(a_sh.h, b_sh + qg * 16)
                                v.tensor_mul(osh64[:], pb[7].ap()[0:64, :],
                                             rb_sb[:, 1, :]).then_inc(a_ow.h, 1)
                    a_rp.v += NG * 2
                    a_rbe.v += NG * 2
                    a_ow.v += NG * 2


        def emit_oproj():
            b_ow_, b_owf = o_w.v, o_wf.v
            b_omm, b_oxu = o_mm.v, o_xu.v
            with nc.Block() as blk:
                @blk.sync
                def _(sync):
                    for do in range(NC8):
                        if do >= 2:
                            sync.wait_ge(o_mm.h, b_omm + (do - 1) * NG)
                        sync.dma_start(out=wos[:, do % 2, :],
                                       in_=wo_d[do].rearrange("p a b -> p (a b)")).then_inc(o_w.h, 16)
                    o_w.v += NC8 * 16

                @blk.tensor
                def _(t):
                    a_sh.wait(t)
                    for do in range(NC8):
                        t.wait_ge(o_w.h, b_ow_ + (do + 1) * 16)
                        for g in range(NG):
                            gs = slice(g * 512, (g + 1) * 512)
                            i = do * NG + g
                            if i >= 2:
                                t.wait_ge(o_xu.h, b_oxu + i - 1)
                            for ic in range(NC8):
                                mm = t.matmul(pb[i % 2][:],
                                              wos[:, do % 2, ic * 128:(ic + 1) * 128],
                                              bufY[:, ic, gs],
                                              start=(ic == 0), stop=(ic == NC8 - 1))
                            mm.then_inc(o_mm.h, 1)
                    o_mm.v += NC8 * NG

                @blk.vector
                def _(v):
                    for do in range(NC8):
                        for g in range(NG):
                            gs = slice(g * 512, (g + 1) * 512)
                            i = do * NG + g
                            v.wait_ge(o_mm.h, b_omm + i + 1)
                            v.tensor_add(bufX[:, do, gs], bufX[:, do, gs],
                                         pb[i % 2][:]).then_inc(o_xu.h, 1)
                    o_xu.v += NC8 * NG

        emit_ln(bufX, bufY, need_ldx=True)
        if "y1" in tap_d:
            _tap_buf(nc, tap_d["y1"], bufY, [], st_out)
        if phases >= 2:
            emit_ffn(bufX, bufY)
            if "tw" in tap_d:
                with nc.Block() as b:
                    @b.sync
                    def _(g):
                        st_out.inc(g.dma_start(
                            out=tap_d["tw"].rearrange("p a b -> p (a b)"),
                            in_=tw_sb[:, :, :].rearrange("p a b -> p (a b)")), 16)
            if "xmid" in tap_d:
                _tap_buf(nc, tap_d["xmid"], bufX, [], st_out)
        if phases >= 3:
            emit_ln(bufX, bufY)
            if "y2" in tap_d:
                _tap_buf(nc, tap_d["y2"], bufY, [], st_out)
            emit_qkv(bufY)
            for tname, tsrc in (("q", qT_d), ("kv", kvin_d), ("kvout", kvout_d)):
                if tname in tap_d:
                    with nc.Block() as b:
                        @b.sync
                        def _(g, tn=tname, ts_=tsrc):
                            st_out.inc(g.dma_start(out=tap_d[tn], in_=ts_), 16)
            if "mneg" in tap_d:
                with nc.Block() as b:
                    @b.sync
                    def _(g):
                        st_out.inc(g.dma_start(out=tap_d["mneg"], in_=mneg[:]), 16)
        if phases >= 4:
            for hp in range(NH // 2):
                emit_attn(hp)
            if "oT" in tap_d:
                _tap_buf(nc, tap_d["oT"], bufY, [], st_out)
        if phases >= 5:
            emit_oproj()
            if "xattn" in tap_d:
                _tap_buf(nc, tap_d["xattn"], bufX, [], st_out)
        if phases >= 6:
            emit_ln(bufX, bufY)
            emit_ffn(bufX, bufY)

        with nc.Block() as b:
            @b.sync
            def _(sy):
                for kc in range(NC8):
                    st_out.inc(sy.dma_start(out=out_d[kc * 128:(kc + 1) * 128, :],
                                            in_=bufX[:, kc, :]), 16)
                st_out.wait(sy)

    return nc, groups


def _tap_buf(nc, dram, buf, afters, st):
    with nc.Block() as b:
        @b.sync
        def _(g):
            for kc in range(NC8):
                st.inc(g.dma_start(out=dram[kc * 128:(kc + 1) * 128, :],
                                   in_=buf[:, kc, :]), 16)


def make_host_inputs(inputs, n_cores=8):
    w_q = np.asarray(inputs["L0_attn_w_q"])
    w_k = np.asarray(inputs["L0_attn_w_k"])
    w_v = np.asarray(inputs["L0_attn_w_v"])
    w_o = np.asarray(inputs["L0_attn_w_o"])
    rw = np.asarray(inputs["L0_ffn_router_w"])
    wg = np.asarray(inputs["L0_ffn_w_gate"])
    wu = np.asarray(inputs["L0_ffn_w_up"])
    wd = np.asarray(inputs["L0_ffn_w_down"])
    sg = np.asarray(inputs["L0_ffn_ws_gate"])
    su = np.asarray(inputs["L0_ffn_ws_up"])
    sd = np.asarray(inputs["L0_ffn_ws_down"])

    def stat_tile(w):
        K, M = w.shape
        return np.ascontiguousarray(
            w.reshape(K // 128, 128, M // 128, 128).transpose(2, 1, 0, 3))

    def mov_tile(w):
        K, N = w.shape
        return np.ascontiguousarray(w.reshape(K // 128, 128, N))

    def gu_tile(wmat):
        K, Fdim = wmat.shape
        return wmat.reshape(8, 128, Fdim // 128, 128).transpose(2, 1, 0, 3)

    wg_t = np.empty((NXP, NFC, 128, 8, 128), np.float32)
    wu_t = np.empty((NXP, NFC, 128, 8, 128), np.float32)
    wd_t = np.empty((NXP, 128, NFC, 1024), np.float32)
    for e in range(NE):
        wg_t[e] = gu_tile(wg[e])
        wu_t[e] = gu_tile(wu[e])
        wd_t[e] = wd[e].reshape(NFC, 128, 1024).transpose(1, 0, 2)
    sg_tiled = gu_tile(sg)
    su_tiled = gu_tile(su)
    for h in range(2):
        wg_t[NE + h] = sg_tiled[h * 4:(h + 1) * 4]
        wu_t[NE + h] = su_tiled[h * 4:(h + 1) * 4]
        wd_t[NE + h] = sd[h * 512:(h + 1) * 512].reshape(NFC, 128, 1024).transpose(1, 0, 2)

    com = {
        "wq_t": np.ascontiguousarray(stat_tile(np.ascontiguousarray(w_q.T))),
        "wo_t": np.ascontiguousarray(stat_tile(np.ascontiguousarray(w_o.T))),
        "wk_m": mov_tile(np.ascontiguousarray(w_k.T)),
        "wv_m": mov_tile(np.ascontiguousarray(w_v.T)),
        "rw_t": np.ascontiguousarray(rw.reshape(8, 128, 8).transpose(1, 0, 2)),
        "wg_t": wg_t, "wu_t": wu_t, "wd_t": wd_t,
        "ones2": np.full((2, 128), 0.5, np.float32),
        "onesc": np.ones((128, 2), np.float32),
        "e2": np.ascontiguousarray(
            (np.arange(128)[:, None] // 64 == np.arange(2)[None, :]).astype(np.float32)),
        "ident": np.eye(128, dtype=np.float32),
    }
    invf = (1.0 / np.float32(10000.0) **
            (np.arange(0, 64, 2, dtype=np.float32) / np.float32(64.0)))
    posf = np.arange(SEQ, dtype=np.float32)
    frq = np.outer(posf, invf).astype(np.float32)
    emb = np.concatenate([frq, frq], axis=-1)
    cosA = np.cos(emb).astype(np.float32)
    sinA = np.sin(emb).astype(np.float32)
    sgn = np.concatenate([-np.ones(32, np.float32), np.ones(32, np.float32)])

    x = np.asarray(inputs["L0_attn_hidden_states"])
    per_core = []
    for c in range(n_cores):
        bidx, half = c // 2, c % 2
        m = dict(com)
        m["xT"] = np.ascontiguousarray(x[bidx, half * T:(half + 1) * T, :].T)
        cosH = cosA[half * T:(half + 1) * T]
        sinS = sinA[half * T:(half + 1) * T] * sgn[None, :]
        m["ropeKc"] = np.ascontiguousarray(
            cosH.reshape(16, 128, 64).transpose(1, 0, 2).reshape(128, 1024))
        m["ropeKs"] = np.ascontiguousarray(
            sinS.reshape(16, 128, 64).transpose(1, 0, 2).reshape(128, 1024))
        m["ropeQc"] = np.ascontiguousarray(
            np.tile(cosH.T, (2, 1)).reshape(128, 4, 512).transpose(1, 0, 2))
        m["ropeQs"] = np.ascontiguousarray(
            np.tile(sinS.T, (2, 1)).reshape(128, 4, 512).transpose(1, 0, 2))
        per_core.append(m)
    return per_core


def kernel(**inputs):
    nc, _ = build_program(n_cores=8, taps=DEBUG_TAPS, phases=PHASES)
    in_maps = make_host_inputs(inputs)
    res = run_bass_kernel_spmd(nc, in_maps, core_ids=list(range(8)))
    out = np.empty((4, 4096, 1024), np.float32)
    for c in range(8):
        bidx, half = c // 2, c % 2
        out[bidx, half * T:(half + 1) * T, :] = res.results[c]["outT"].T
    return out
